# version 02 (final)
# baseline (speedup 1.0000x reference)
import numpy as np
from contextlib import ExitStack

import concourse.bass as bass
import concourse.mybir as mybir
from concourse.bass_utils import run_bass_kernel_spmd
from concourse.bass_core_rust import IndirectOffsetOnAxis

F32 = mybir.dt.float32
BF16 = mybir.dt.bfloat16
I32 = mybir.dt.int32
ALU = mybir.AluOpType
AF = mybir.ActivationFunctionType
AX = mybir.AxisListType

D = 1024
S = 2048
NB = 8
NH = 16
HD = 64
PT = D // 128
NT = S // 128
NQC = S // 512
NE = 8
CAP = 2560
CLOC = 1024
IS = 8192
IR = 4096
DI = 2048
IQ = 2816
LN_EPS = 1e-5
RMS_EPS = float(np.finfo(np.float32).eps)

ENG_SEM = {"tensor": "tns", "vector": "vec", "scalar": "act", "gpsimd": "gps", "sync": "syn"}


def tile_lhsT(W):
    K, M = W.shape
    kc, mc = K // 128, M // 128
    return np.ascontiguousarray(
        W.reshape(kc, 128, mc, 128).transpose(2, 1, 0, 3).reshape(mc, 128, kc * 128))


class MK:
    def __init__(self, nc, ctx):
        self.nc = nc
        self.ctx = ctx
        self.sems = {}
        self.cnt = {}
        self.bufs = {}
        self.host = {}
        self.host_percore = {}
        self.dram = {}
        self.enable_taps = False
        self.banks = {}
        self.engq = {id(getattr(nc, e)): "dma_" + e
                     for e in ("sync", "gpsimd", "vector", "scalar", "tensor")}

    def sem(self, name):
        if name not in self.sems:
            self.sems[name] = self.ctx.enter_context(self.nc.semaphore(name))
            self.cnt[name] = 0
        return self.sems[name]

    def inc(self, instr, name, n=1):
        instr.then_inc(self.sem(name), n)
        self.cnt[name] += n
        return instr

    def mark(self, instr, eng_name):
        nm = ENG_SEM[eng_name]
        self.inc(instr, nm, 1)
        return (nm, self.cnt[nm])

    def dma(self, eng, out, in_):
        qn = self.engq[id(eng)]
        instr = eng.dma_start(out=out, in_=in_)
        self.inc(instr, qn, 16)
        return (qn, self.cnt[qn])

    def waitt(self, eng, tok):
        if tok is not None:
            eng.wait_ge(self.sem(tok[0]), tok[1])

    def sb(self, name, shape, dt):
        if name not in self.bufs:
            self.bufs[name] = self.ctx.enter_context(self.nc.sbuf_tensor(name, shape, dt))
        return self.bufs[name]

    def ps_bank(self, name):
        if name not in self.bufs:
            self.bufs[name] = self.ctx.enter_context(self.nc.psum_tensor(name, [128, 512], F32))
            self.banks[name] = {"w": None, "r": None}
        return self.bufs[name]

    def bw(self, name, emit_group):
        self.ps_bank(name)
        self.waitt(self.nc.tensor, self.banks[name]["r"])
        last = emit_group()
        self.banks[name]["w"] = self.mark(last, "tensor")
        return self.banks[name]["w"]

    def br(self, name, eng_name, emit_read):
        eng = getattr(self.nc, eng_name)
        self.waitt(eng, self.banks[name]["w"])
        last = emit_read()
        self.banks[name]["r"] = self.mark(last, eng_name)
        return self.banks[name]["r"]

    def din(self, name, arr=None, percore=None, dt=F32):
        if name not in self.dram:
            if percore is not None:
                shape = list(percore[0].shape)
                self.host_percore[name] = [np.ascontiguousarray(a) for a in percore]
            else:
                shape = list(arr.shape)
                self.host[name] = np.ascontiguousarray(arr)
            self.dram[name] = self.nc.dram_tensor(name, shape, dt, kind="ExternalInput").ap()
        return self.dram[name]

    def dmaT(self, eng, out, in_):
        qn = self.engq[id(eng)]
        instr = eng.dma_start_transpose(out, in_)
        self.inc(instr, qn, 16)
        return (qn, self.cnt[qn])

    def dmaI(self, eng, out, out_off, in_, in_off):
        qn = self.engq[id(eng)]
        instr = eng.indirect_dma_start(out, out_off, in_, in_off)
        self.inc(instr, qn, 16)
        return (qn, self.cnt[qn])

    def dout(self, name, shape, dt=F32):
        if name not in self.dram:
            self.dram[name] = self.nc.dram_tensor(name, shape, dt, kind="ExternalOutput").ap()
        return self.dram[name]

    def dscr(self, name, shape, dt=F32):
        if name not in self.dram:
            kind = "ExternalOutput" if self.enable_taps else "Internal"
            self.dram[name] = self.nc.dram_tensor(name, shape, dt, kind=kind).ap()
        return self.dram[name]

    def barrier(self):
        nc = self.nc
        b = self.sem("bar")
        engs = (nc.tensor, nc.vector, nc.scalar, nc.gpsimd, nc.sync)
        for qn in self.engq.values():
            if qn in self.cnt:
                nc.sync.wait_ge(self.sem(qn), self.cnt[qn])
        for eng in engs:
            eng.nop().then_inc(b, 1)
        self.cnt["bar"] += 5
        for eng in engs:
            eng.wait_ge(b, self.cnt["bar"])

    def tap(self, name, sbuf_ap):
        if not self.enable_taps:
            return
        d = self.dout("tap_" + name, list(sbuf_ap.shape), sbuf_ap.dtype)
        self.barrier()
        self.dma(self.nc.sync, d[:], sbuf_ap)
        self.barrier()

    def scra(self, lo, cols):
        a = self.sb("SCRA", [128, 18432], BF16)
        return a[:, lo:lo + cols]

    def scraf(self, lo_f32, cols):
        a = self.sb("SCRA", [128, 18432], BF16)
        return a[:].bitcast(F32)[:, lo_f32:lo_f32 + cols]

    def mm(self, out_ap, tiles, tile_position=None):
        n = len(tiles)
        last = None
        for i, (lt, rt) in enumerate(tiles):
            last = self.nc.tensor.matmul(out_ap, lt, rt, start=(i == 0), stop=(i == n - 1),
                                         tile_position=tile_position)
        return last



def host_prepare(inputs):
    h = {}
    pc = {}
    x = np.ascontiguousarray(inputs["L0_attn_x"])
    perms = [[bb for bb in range(NB) if bb != b] + [b] for b in range(NB)]
    x_fm = np.ascontiguousarray(np.swapaxes(x, 1, 2))
    pc["x_all_fm"] = [np.ascontiguousarray(x_fm[p]) for p in perms]
    pc["mask_before"] = [
        np.ascontiguousarray(np.repeat(np.array([[1.0 if perms[b][i] < b else 0.0
                                                  for i in range(NB)]], np.float32), NB, axis=0))
        for b in range(NB)]
    for li, (ck, wk) in enumerate((("L0_c", "L0_wmod"), ("L1_c", "L1_wmod"))):
        cT = inputs[ck].T
        pc[f"l{li}_cT"] = [np.ascontiguousarray(cT[:, p]) for p in perms]
        h[f"l{li}_wmodt"] = tile_lhsT(inputs[wk])
    h["Wqkvt"] = tile_lhsT(inputs["L0_attn_W_w"].T)
    h["qkv_bT"] = inputs["L0_attn_W_b"].reshape(24, 128).T
    h["projWt"] = tile_lhsT(inputs["L0_attn_proj_w"].T)
    h["vbias_row"] = inputs["L0_attn_W_b"][2048:3072].reshape(1, 1024)
    h["proj_bT"] = inputs["L0_attn_proj_b"].reshape(8, 128).T
    h["ln_gT"] = inputs["L0_attn_ln_g"].reshape(8, 128).T
    h["ln_bT"] = inputs["L0_attn_ln_b"].reshape(8, 128).T
    rpb = inputs["L0_attn_rpb"]
    dd = np.arange(16)[:, None, None]
    kk = np.arange(128)[None, :, None]
    qq = np.arange(128)[None, None, :]
    idx = 2047 + kk - qq - 128 * dd
    h["rpbT_tiles"] = np.ascontiguousarray(
        rpb[idx, :].transpose(3, 1, 0, 2).reshape(NH, 128, 16 * 128))
    h["rpb_absT"] = np.ascontiguousarray(np.abs(rpb).T)
    q_ = np.arange(128)
    h["tri_qk"] = (q_[None, :] >= q_[:, None]).astype(np.float32)
    h["router_wt"] = np.ascontiguousarray(inputs["L0_ffn_router_w"].T)
    h["sgt"] = tile_lhsT(inputs["L0_ffn_sg"])
    h["sut"] = tile_lhsT(inputs["L0_ffn_su"])
    h["sdt"] = tile_lhsT(inputs["L0_ffn_sd"])
    h["rgt"] = np.stack([tile_lhsT(inputs["L0_ffn_rg"][e]) for e in range(NE)])
    h["rut"] = np.stack([tile_lhsT(inputs["L0_ffn_ru"][e]) for e in range(NE)])
    h["rdt"] = np.stack([tile_lhsT(inputs["L0_ffn_rd"][e]) for e in range(NE)])
    h["W_int"] = tile_lhsT(inputs["L1_attn_W_in"])
    h["b_in_row"] = inputs["L1_attn_b_in"].reshape(1, DI)
    h["conv_wT"] = np.ascontiguousarray(inputs["L1_attn_conv_w"][:, 0, :].T)
    cw = inputs["L1_attn_conv_w"][:, 0, :]
    h["conv_wF"] = np.ascontiguousarray(cw.reshape(16, 128, 4))
    h["conv_bF"] = np.ascontiguousarray(inputs["L1_attn_conv_b"].reshape(16, 128, 1))
    h["b_inF"] = np.ascontiguousarray(inputs["L1_attn_b_in"].reshape(16, 128).T)
    h["Bp_col"] = inputs["L1_attn_Bp"].reshape(16, 1)
    h["Cp_col"] = inputs["L1_attn_Cp"].reshape(16, 1)
    h["conv_b_row"] = inputs["L1_attn_conv_b"].reshape(1, DI)
    h["A"] = inputs["L1_attn_A"]
    h["Bp_row"] = inputs["L1_attn_Bp"].reshape(1, 16)
    h["Cp_row"] = inputs["L1_attn_Cp"].reshape(1, 16)
    h["Dp"] = inputs["L1_attn_Dp"].reshape(1, 1)
    h["W_outt"] = tile_lhsT(inputs["L1_attn_W_out"])
    h["b_outT"] = inputs["L1_attn_b_out"].reshape(8, 128).T
    h["w_gate"] = inputs["L1_ffn_w_gate"]
    h["w_up"] = inputs["L1_ffn_w_up"]
    h["w_down"] = inputs["L1_ffn_w_down"]
    h["ident128"] = np.eye(128, dtype=np.float32)
    return h, pc, perms



def emit_mod_prep(mk: MK):
    nc = mk.nc
    for li in range(2):
        cT = mk.din(f"l{li}_cT", percore=mk.host_percore[f"l{li}_cT"])
        wmodt = mk.din(f"l{li}_wmodt", mk.host[f"l{li}_wmodt"])
        modT = mk.sb(f"modT{li}", [128, 48 * 8], F32)
        csl = mk.sb("mp_csl", [128, 8 * 8], F32)
        chib = mk.sb("mp_chib", [128, 8 * 8], BF16)
        chif = mk.sb("mp_chif", [128, 8 * 8], F32)
        clob = mk.sb("mp_clob", [128, 8 * 8], BF16)
        wsb = mk.scraf(0, 2048)
        whib = mk.scra(4096, 2048)
        wlob = mk.scra(6144, 2048)
        whif = mk.scraf(4096, 1024)
        ldc = []
        for p in range(PT):
            ldc.append(mk.dma(nc.sync, csl[:, p * 8:(p + 1) * 8], cT[p * 128:(p + 1) * 128, :]))
        mk.waitt(nc.scalar, ldc[-1])
        silu_tok = mk.mark(nc.scalar.activation(out=csl[:], in_=csl[:], func=AF.Silu), "scalar")
        mk.waitt(nc.scalar, silu_tok)
        t = mk.mark(nc.scalar.activation(out=chib[:], in_=csl[:], func=AF.Identity), "scalar")
        mk.waitt(nc.scalar, t)
        t = mk.mark(nc.scalar.activation(out=chif[:], in_=chib[:], func=AF.Identity), "scalar")
        mk.waitt(nc.vector, t)
        ctok = mk.mark(nc.vector.tensor_tensor(clob[:], csl[:], chif[:], ALU.subtract),
                       "vector")
        mm_r = {}
        tt_tok = {}
        for m in range(48):
            o = (m % 2) * 1024
            mk.waitt(nc.sync, tt_tok.get(m - 2))
            ld = mk.dma(nc.sync, wsb[:, o:o + 1024], wmodt[m])
            mk.waitt(nc.scalar, ld)
            mk.waitt(nc.scalar, mm_r.get(m - 2))
            mk.waitt(nc.scalar, tt_tok.get(m - 1))
            a1 = mk.mark(nc.scalar.activation(out=whib[:, o:o + 1024], in_=wsb[:, o:o + 1024],
                                              func=AF.Identity), "scalar")
            a2 = mk.mark(nc.scalar.activation(out=whif[:], in_=whib[:, o:o + 1024],
                                              func=AF.Identity), "scalar")
            mk.waitt(nc.vector, a2)
            mk.waitt(nc.vector, mm_r.get(m - 2))
            tt_tok[m] = mk.mark(nc.vector.tensor_tensor(wlob[:, o:o + 1024], wsb[:, o:o + 1024],
                                                        whif[:], ALU.subtract), "vector")
            mk.waitt(nc.tensor, tt_tok[m])
            if m == 0:
                mk.waitt(nc.tensor, ctok)

            def grp(o=o):
                tiles = []
                for wa, ca in ((whib, chib), (wlob, chib), (whib, clob)):
                    tiles += [(wa[:, o + p * 128:o + (p + 1) * 128], ca[:, p * 8:(p + 1) * 8])
                              for p in range(PT)]
                return mk.mm(mk.ps_bank("pA")[:, 0:8], tiles)
            mk.bw("pA", grp)
            mk.br("pA", "vector",
                  lambda m=m: nc.vector.tensor_copy(modT[:, m * 8:(m + 1) * 8],
                                                    mk.ps_bank("pA")[:, 0:8]))
            mm_r[m] = mk.banks["pA"]["r"]
        mk.barrier()
        for k in (1, 4):
            blk = modT[:, (k * 8) * 8:(k * 8 + 8) * 8]
            nc.vector.tensor_scalar_add(blk, blk, 1.0)
        for k in (2, 5):
            sgn = mk.sb(f"mp_sgn{k}", [128, 64], F32)
            blk = modT[:, (k * 8) * 8:(k * 8 + 8) * 8]
            nc.scalar.activation(out=sgn[:], in_=blk, func=AF.Sign)
            t = mk.mark(nc.scalar.activation(out=blk, in_=blk, func=AF.Abs), "scalar")
            mk.waitt(nc.vector, t)
            t = mk.mark(nc.vector.tensor_scalar_max(blk, blk, 0.25), "vector")
            mk.waitt(nc.vector, t)
            nc.vector.tensor_tensor(blk, blk, sgn[:], ALU.mult)
        mk.barrier()


def pp(mk, li, k, p, slot):
    c = (k * 8 + p) * 8 + slot
    return mk.bufs[f"modT{li}"][:, c:c + 1]



def emit_consts(mk: MK):
    nc = mk.nc
    ones1b = mk.sb("c_ones1b", [128, 1], BF16)
    ones1f = mk.sb("c_ones1f", [128, 1], F32)
    onesr = mk.sb("c_onesr", [1, 128], F32)
    onesrow = mk.sb("c_onesrow", [1, 128], BF16)
    negr = mk.sb("c_negr", [1, 128], F32)
    nc.vector.memset(ones1b[:], 1.0)
    nc.vector.memset(ones1f[:], 1.0)
    nc.vector.memset(onesr[:], 1.0)
    nc.vector.memset(onesrow[:], 1.0)
    nc.vector.memset(negr[:], -1.0)
    ident = mk.din("ident128", mk.host["ident128"])
    idsb = mk.sb("c_ident", [128, 128], BF16)
    mk.dma(nc.gpsimd, idsb[:], ident[:])
    idf = mk.sb("c_identf", [128, 128], F32)
    mk.dma(nc.sync, idf[:], ident[:])
    epsln = mk.sb("c_epsln", [128, 1], F32)
    nc.vector.memset(epsln[:], LN_EPS)
    epsrms = mk.sb("c_epsrms", [128, 1], F32)
    nc.vector.memset(epsrms[:], RMS_EPS)
    mk.barrier()


def emit_stats(mk: MK, src_tiles, eps, tag):
    nc = mk.nc
    sq = mk.scra(0, 1024)
    ones1b = mk.bufs["c_ones1b"]
    ones1f = mk.bufs["c_ones1f"]
    onesr = mk.bufs["c_onesr"]
    rows = mk.scraf(512, 3 * S)[0:1, :]
    stm = rows[:, 0:S]
    str_ = rows[:, S:2 * S]
    tmp = rows[:, 2 * S:3 * S]
    rsbc = mk.sb(f"rsbc_{tag}", [128, S], BF16)
    mrsbc = mk.sb(f"mrsbc_{tag}", [128, S], BF16)
    is_f32 = src_tiles[0].dtype == F32
    one = ones1f if is_f32 else ones1b
    for c in range(NQC):
        sl = slice(c * 512, (c + 1) * 512)
        pa = mk.ps_bank("pA")
        pb = mk.ps_bank("pB")
        mk.bw("pA", lambda pa=pa, sl=sl: mk.mm(pa[0:1, 0:512],
                                               [(one[:], t[:, sl]) for t in src_tiles]))
        mm_tok = {}
        mk.waitt(nc.tensor, mk.banks["pB"]["r"])
        for i, t in enumerate(src_tiles):
            so = (i % 2) * 512
            mk.waitt(nc.scalar, mm_tok.get(i - 2))
            stk = mk.mark(nc.scalar.activation(out=sq[:, so:so + 512], in_=t[:, sl],
                                               func=AF.Square), "scalar")
            mk.waitt(nc.tensor, stk)
            last = nc.tensor.matmul(pb[0:1, 0:512], ones1b[:], sq[:, so:so + 512],
                                    start=(i == 0), stop=(i == PT - 1))
            mm_tok[i] = mk.mark(last, "tensor")
        mk.banks["pB"]["w"] = mm_tok[PT - 1]
        mk.br("pA", "scalar", lambda pa=pa, sl=sl: nc.scalar.copy(out=stm[:, sl],
                                                                  in_=pa[0:1, 0:512]))
        mk.br("pB", "scalar", lambda pb=pb, sl=sl: nc.scalar.copy(out=str_[:, sl],
                                                                  in_=pb[0:1, 0:512]))
    at = ("act", mk.cnt["act"])
    mk.waitt(nc.vector, at)
    nc.vector.tensor_scalar_mul(stm, stm, 1.0 / D)
    nc.vector.tensor_scalar_mul(str_, str_, 1.0 / D)
    nc.vector.tensor_tensor(tmp, stm, stm, ALU.mult)
    vt = mk.mark(nc.vector.tensor_tensor(str_, str_, tmp, ALU.subtract), "vector")
    mk.waitt(nc.scalar, vt)
    ebuf = mk.bufs["c_epsln"] if eps == LN_EPS else mk.bufs["c_epsrms"]
    nc.scalar.activation(out=str_, in_=str_, func=AF.Ln, bias=ebuf[0:1, :])
    at = mk.mark(nc.scalar.activation(out=str_, in_=str_, func=AF.Exp, scale=-0.5),
                 "scalar")
    mk.waitt(nc.vector, at)
    vt = mk.mark(nc.vector.tensor_tensor(tmp, stm, str_, ALU.mult), "vector")
    mk.waitt(nc.tensor, vt)
    for c in range(NQC):
        sl = slice(c * 512, (c + 1) * 512)
        for bank, row, dst in (("pA", str_[:, sl], rsbc), ("pB", tmp[:, sl], mrsbc)):
            pz = mk.ps_bank(bank)
            mk.bw(bank, lambda pz=pz, row=row: mk.mm(pz[:, 0:512], [(onesr[:], row)]))
            mk.br(bank, "scalar",
                  lambda pz=pz, dst=dst, sl=sl: nc.scalar.copy(out=dst[:, sl], in_=pz[:, 0:512]))
    mk.barrier()


def emit_modln(mk: MK, src_tiles, tag, li, kc, ks, slot, out):
    nc = mk.nc
    rsbc = mk.bufs[f"rsbc_{tag}"]
    mrsbc = mk.bufs[f"mrsbc_{tag}"]
    u = mk.scra(0, S)
    t_ = mk.scra(2048, S)
    toks = {}
    for p in range(PT):
        nc.vector.tensor_tensor(u, src_tiles[p], rsbc[:], ALU.mult)
        vt = mk.mark(nc.vector.tensor_tensor(t_, u, mrsbc[:], ALU.subtract), "vector")
        mk.waitt(nc.scalar, vt)
        at = mk.mark(nc.scalar.activation(out=out[:, p * S:(p + 1) * S], in_=t_,
                                          func=AF.Identity, scale=pp(mk, li, kc, p, slot),
                                          bias=pp(mk, li, ks, p, slot)), "scalar")
        mk.waitt(nc.vector, at)
    mk.barrier()



def emit_attn_prep_consts(mk: MK):
    nc = mk.nc
    rpb_absT = mk.din("rpb_absT", mk.host["rpb_absT"])
    rpbm = mk.sb("at_rpbm", [16, 1], F32)
    rpbm_row = mk.sb("at_rpbm_row", [1, 16], F32)
    rtmp = mk.scra(0, 4095)[0:16, :]
    t = mk.dma(nc.gpsimd, rtmp, rpb_absT[:])
    mk.waitt(nc.vector, t)
    vt = mk.mark(nc.vector.tensor_reduce(rpbm[:], rtmp, AX.X, ALU.max), "vector")
    mk.waitt(nc.sync, vt)
    mk.dma(nc.sync, rpbm_row[:], rpbm[:])
    qbT = mk.din("qkv_bT", mk.host["qkv_bT"])
    qb = mk.sb("at_qb", [128, 24], F32)
    t = mk.dma(nc.sync, qb[:], qbT[:])
    qbq = mk.sb("at_qbq", [128, 8], F32)
    mk.waitt(nc.scalar, t)
    mk.mark(nc.scalar.activation(out=qbq[:], in_=qb[:, 0:8], func=AF.Identity,
                                 scale=1.0 / 32.0), "scalar")
    tri = mk.din("tri_qk", mk.host["tri_qk"])
    trisb = mk.sb("at_tri", [128, 128], BF16)
    mk.dma(nc.gpsimd, trisb[:], tri[:])
    mk.barrier()
    mk.tap("qbq", mk.bufs["at_qbq"][:])


def emit_qkv_headpair(mk: MK, hp, h1, wq_sb, ld_tok):
    nc = mk.nc
    qa = mk.scra(8960, S)
    ka = mk.scra(11008, S)
    va = mk.scra(0, 4096)
    qb = mk.bufs["at_qb"]
    mk.waitt(nc.tensor, ld_tok)
    for qc in range(NQC):
        sl = slice(qc * 512, (qc + 1) * 512)
        for which, dst, bias_col in ((0, qa, hp), (1, ka, 8 + hp)):
            bank = ("pC", "pD")[which]
            pb = mk.ps_bank(bank)
            wof = which * 1024
            mk.bw(bank, lambda pb=pb, wof=wof, sl=sl: mk.mm(
                pb[:, 0:512],
                [(wq_sb[:, wof + p * 128:wof + (p + 1) * 128], h1[:, p * S + sl.start:p * S + sl.stop])
                 for p in range(PT)]))

            def ev(pb=pb, dst=dst, bias_col=bias_col, sl=sl):
                bc = qb[:, bias_col:bias_col + 1]
                nc.scalar.activation(out=dst[0:64, sl], in_=pb[0:64, 0:512],
                                     func=AF.Identity, bias=bc[0:64, :])
                return nc.scalar.activation(out=dst[64:128, sl], in_=pb[64:128, 0:512],
                                            func=AF.Identity, bias=bc[64:128, :])
            mk.br(bank, "scalar", ev)
    vb_row = mk.din("vbias_row", mk.host["vbias_row"])
    vbr = mk.sb("at_vbr", [1, 1024], BF16)
    if not hasattr(mk, "_vbr_tok"):
        mk._vbr_tok = mk.dma(nc.gpsimd, vbr[:], vb_row[:])
    mk.waitt(nc.tensor, mk._vbr_tok)
    for kt in range(NT):
        bank = ("pC", "pD")[kt % 2]
        pb = mk.ps_bank(bank)

        def vgrp(pb=pb, kt=kt):
            last = None
            for p in range(PT):
                last = nc.tensor.matmul(pb[:, 0:128],
                                        h1[:, p * S + kt * 128:p * S + (kt + 1) * 128],
                                        wq_sb[:, 2048 + p * 128:2048 + (p + 1) * 128],
                                        start=(p == 0), stop=False)
            return nc.tensor.matmul(pb[:, 0:128],
                                    mk.bufs["c_onesrow"][:, 0:128],
                                    vbr[:, hp * 128:(hp + 1) * 128], start=False, stop=True)
        mk.bw(bank, vgrp)

        def vev(pb=pb, kt=kt):
            nc.scalar.copy(out=va[:, kt * 128:kt * 128 + 64], in_=pb[:, 0:64])
            return nc.scalar.copy(out=va[:, 2048 + kt * 128 + 64:2048 + (kt + 1) * 128],
                                  in_=pb[:, 64:128])
        mk.br(bank, "scalar", vev)
    mk.barrier()


def emit_va_ones(mk: MK):
    va = mk.scra(0, 4096)
    g = mk.nc.gpsimd
    for kt in range(NT):
        g.memset(va[:, 0 * 2048 + kt * 128 + 64:0 * 2048 + (kt + 1) * 128], 1.0)
        g.memset(va[:, 1 * 2048 + kt * 128:1 * 2048 + kt * 128 + 64], 1.0)
    mk.mark(g.engine_nop() if hasattr(g, "engine_nop") else g.nop(), "gpsimd")
    mk.barrier()


def emit_chead(mk: MK, hp):
    nc = mk.nc
    qa = mk.scra(8960, S)
    ka = mk.scra(11008, S)
    sq = mk.scra(13056, 4096)
    nct = mk.sb("at_nct", [128, 64 + 4], F32)
    red4 = mk.sb("at_red4", [4, 128], F32)
    redr = mk.sb("at_redr", [1, 4], F32)
    cc = mk.sb("at_cc", [1, 2], F32)
    negc128 = mk.sb("at_negc", [128, 2], F32)
    ones1b = mk.bufs["c_ones1b"]
    onesr = mk.bufs["c_onesr"]
    identf = mk.bufs["c_identf"]
    a1 = mk.mark(nc.scalar.activation(out=sq[:, 0:S], in_=qa[:], func=AF.Square), "scalar")
    a2 = mk.mark(nc.scalar.activation(out=sq[:, S:2 * S], in_=ka[:], func=AF.Square), "scalar")
    pe = mk.ps_bank("pE")
    mk.waitt(nc.tensor, a2)

    def ngrp():
        last = None
        for par, (lo, hi) in enumerate(((0, 64), (64, 128))):
            for t in range(NT):
                last = nc.tensor.matmul(pe[:, par * 32 + t:par * 32 + t + 1],
                                        sq[lo:hi, t * 128:(t + 1) * 128],
                                        ones1b[lo:hi, :], start=True, stop=True)
                last = nc.tensor.matmul(pe[:, par * 32 + NT + t:par * 32 + NT + t + 1],
                                        sq[lo:hi, S + t * 128:S + (t + 1) * 128],
                                        ones1b[lo:hi, :], start=True, stop=True)
        return last
    mk.bw("pE", ngrp)
    mk.br("pE", "scalar", lambda: nc.scalar.copy(out=nct[:, 0:64], in_=pe[:, 0:64]))
    at = ("act", mk.cnt["act"])
    mk.waitt(nc.vector, at)
    for j in range(4):
        nc.vector.tensor_reduce(nct[:, 64 + j:65 + j], nct[:, j * 16:(j + 1) * 16], AX.X, ALU.max)
    vt = mk.mark(nc.vector.nop(), "vector")
    mk.waitt(nc.tensor, vt)
    pf = mk.ps_bank("pF")
    mk.waitt(nc.tensor, mk.banks["pF"]["r"])
    tp = nc.tensor.transpose(pf[0:4, 0:128], nct[:, 64:68], identf[:])
    mk.banks["pF"]["w"] = mk.mark(tp, "tensor")
    mk.br("pF", "scalar", lambda: nc.scalar.copy(out=red4[:], in_=pf[0:4, 0:128]))
    at = ("act", mk.cnt["act"])
    mk.waitt(nc.vector, at)
    vt = mk.mark(nc.vector.tensor_reduce(red4[:, 0:1], red4[:], AX.X, ALU.max), "vector")
    mk.waitt(nc.sync, vt)
    t = mk.dma(nc.sync, redr[:], red4[:, 0:1])
    mk.waitt(nc.scalar, t)
    at = mk.mark(nc.scalar.activation(out=redr[:], in_=redr[:], func=AF.Sqrt,
                                      scale=1.0 / 32.0), "scalar")
    mk.waitt(nc.vector, at)
    t0 = mk.mark(nc.vector.tensor_tensor(cc[:, 0:1], redr[:, 0:1], redr[:, 1:2],
                                         ALU.mult), "vector")
    mk.waitt(nc.vector, t0)
    t1 = mk.mark(nc.vector.tensor_tensor(cc[:, 1:2], redr[:, 2:3], redr[:, 3:4],
                                         ALU.mult), "vector")
    mk.waitt(nc.vector, t1)
    vt = mk.mark(nc.vector.tensor_tensor(cc[:], cc[:],
                                         mk.bufs["at_rpbm_row"][:, 2 * hp:2 * hp + 2],
                                         ALU.add), "vector")
    mk.waitt(nc.tensor, vt)
    pg = mk.ps_bank("pG")
    mk.waitt(nc.tensor, mk.banks["pG"]["r"])
    mm = nc.tensor.matmul(pg[:, 0:2], mk.bufs["c_negr"][:], cc[:], start=True, stop=True)
    mk.banks["pG"]["w"] = mk.mark(mm, "tensor")
    mk.br("pG", "scalar", lambda: nc.scalar.copy(out=negc128[:], in_=pg[:, 0:2]))
    if hp == 0:
        mk.tap("ch_redr", redr[:])
        mk.tap("ch_cc", cc[:])
        mk.tap("ch_nct", nct[:, 64:68])
    mk.barrier()


def emit_attn_pair(mk: MK, hp, osd):
    nc = mk.nc
    ost = mk.sb("at_ost", [128, 2 * 512], BF16)
    ost_free = getattr(mk, "_ost_free", [None, None])
    mk._ost_free = ost_free
    ev_done = {}
    qa = mk.scra(8960, S)
    ka = mk.scra(11008, S)
    va = mk.scra(0, 4096)
    eb = mk.scra(4096, 4864)
    negc = mk.bufs["at_negc"]
    p0 = mk.scra(13056, 2048)
    pt = mk.scra(15104, 2048)
    rsb = mk.sb("at_rsb", [128, 512], F32)
    rsb2 = mk.sb("at_rsb2", [128, 512], BF16)
    rsbb = mk.scra(17152, 1024)
    steps = [(qc, kt) for qc in range(NQC) for kt in range(qc * 4 + 4)]
    ns = len(steps)
    sE = {}
    sM = {}
    sPV = {}
    pS = [["pA", "pB"], ["pC", "pD"]]
    pO = [["pE", "pF"], ["pG", "pH"]]

    def emit_S(s):
        qc, kt = steps[s]
        i = s % 2
        for par, (lo, hi) in enumerate(((0, 64), (64, 128))):
            bk = pS[par][i]
            psb = mk.ps_bank(bk)
            mk.waitt(nc.tensor, mk.banks[bk]["r"])
            last = nc.tensor.matmul(psb[:, 0:512],
                                    ka[lo:hi, kt * 128:(kt + 1) * 128],
                                    qa[lo:hi, qc * 512:(qc + 1) * 512],
                                    start=True, stop=True,
                                    tile_position=(lo, 0))
            mk.banks[bk]["w"] = mk.mark(last, "tensor")

    def emit_EM(s):
        qc, kt = steps[s]
        i = s % 2
        dslot = qc * 4 - kt + 3
        for par in range(2):
            bk = pS[par][i]
            psb = mk.ps_bank(bk)
            ci = par * 2 + i
            mk.waitt(nc.scalar, mk.banks[bk]["w"])
            mk.waitt(nc.scalar, sM.get((s - 2, par)))
            mk.banks[bk]["r"] = sE[(s, par)] = mk.mark(
                nc.scalar.activation(out=p0[:, ci * 512:(ci + 1) * 512], in_=psb[:, 0:512],
                                     func=AF.Exp, bias=negc[:, par:par + 1],
                                     scale=1.0 / 32.0), "scalar")
            mk.waitt(nc.vector, sE[(s, par)])
            mk.waitt(nc.vector, sPV.get((s - 2, par)))
            ebo = par * (19 * 128)
            sM[(s, par)] = mk.mark(nc.vector.tensor_tensor(
                pt[:, ci * 512:(ci + 1) * 512], p0[:, ci * 512:(ci + 1) * 512],
                eb[:, ebo + dslot * 128: ebo + dslot * 128 + 512], ALU.mult), "vector")

    def emit_evac(qc, par):
        ob = pO[par][qc % 2]
        pob = mk.ps_bank(ob)
        lo, hi = (0, 64) if par == 0 else (64, 128)
        llo, lhi = (64, 128) if par == 0 else (0, 64)
        ro = 0
        so = (qc % 2) * 512
        mk.waitt(nc.scalar, mk.banks[ob]["w"])
        nc.scalar.activation(out=rsb[llo:lhi, ro:ro + 512], in_=pob[llo:lhi, 0:512],
                             func=AF.Ln)
        rt = mk.mark(nc.scalar.activation(out=rsbb[llo:lhi, ro:ro + 512],
                                          in_=rsb[llo:lhi, ro:ro + 512],
                                          func=AF.Exp, scale=-1.0), "scalar")
        mk.waitt(nc.sync, rt)
        mt = mk.dma(nc.sync, rsb2[lo:hi, ro:ro + 512], rsbb[llo:lhi, ro:ro + 512])
        mk.waitt(nc.vector, mt)
        mk.waitt(nc.vector, ost_free[qc % 2])
        ev = mk.mark(nc.vector.tensor_tensor(
            ost[lo:hi, so:so + 512],
            pob[lo:hi, 0:512], rsb2[lo:hi, ro:ro + 512], ALU.mult), "vector")
        mk.banks[ob]["r"] = ev
        ev_done[(qc, par)] = ev
        if par == 1:
            mk.waitt(nc.sync, ev_done[(qc, 0)])
            mk.waitt(nc.sync, ev)
            ost_free[qc % 2] = mk.dma(nc.sync, osd[hp, qc], ost[:, so:so + 512])

    def emit_PV(s):
        qc, kt = steps[s]
        i = s % 2
        for par in range(2):
            ob = pO[par][qc % 2]
            pob = mk.ps_bank(ob)
            ci = par * 2 + i
            vao = par * 2048
            mk.waitt(nc.tensor, sM[(s, par)])
            if kt == 0:
                mk.waitt(nc.tensor, mk.banks[ob]["r"])
            last = nc.tensor.matmul(pob[:, 0:512],
                                    va[:, vao + kt * 128:vao + (kt + 1) * 128],
                                    pt[:, ci * 512:(ci + 1) * 512],
                                    start=(kt == 0), stop=(kt == qc * 4 + 3))
            sPV[(s, par)] = mk.mark(last, "tensor")
            if kt == qc * 4 + 3:
                mk.banks[ob]["w"] = sPV[(s, par)]
                emit_evac(qc, par)

    emit_S(0)
    emit_EM(0)
    for s in range(1, ns):
        emit_S(s)
        emit_EM(s)
        emit_PV(s - 1)
    emit_PV(ns - 1)
    mk.barrier()


def emit_eb(mk: MK, head, par):
    nc = mk.nc
    eb = mk.scra(4096, 4864)
    ebst = mk.sb("at_ebst", [128, 2048], F32)
    rpbT = mk.din("rpbT_tiles", mk.host["rpbT_tiles"])
    ebo = par * (19 * 128)
    if not getattr(mk, "_eb_zeroed", False):
        for q in range(2):
            nc.gpsimd.memset(eb[:, q * (19 * 128):q * (19 * 128) + 384], 0.0)
        mk._eb_zeroed = True
    mk.waitt(nc.sync, getattr(mk, "_eb_consumed", None))
    t = mk.dma(nc.sync, ebst[:], rpbT[head])
    mk.waitt(nc.scalar, t)
    at = mk.mark(nc.scalar.activation(out=eb[:, ebo + 384:ebo + 384 + 2048], in_=ebst[:],
                                      func=AF.Exp), "scalar")
    mk._eb_consumed = at
    mk.waitt(nc.vector, at)
    vt = mk.mark(nc.vector.tensor_tensor(eb[:, ebo + 384:ebo + 384 + 128],
                                         eb[:, ebo + 384:ebo + 384 + 128],
                                         mk.bufs["at_tri"][:], ALU.mult), "vector")
    return vt



def emit_proj_y(mk: MK, slot, osd, h1, ybuf):
    nc = mk.nc
    oslab = mk.scra(0, 8192)
    pwbuf = mk.sb("pj_w", [128, 1024], BF16)
    pbias = mk.bufs["w_projb"]
    projWt = mk.din("projWt", mk.host["projWt"])
    slab_tok = [None, None]
    mm_done = {}
    wt_done = {}
    yt_done = {}
    for qc in range(NQC):
        so = (qc % 2) * PT * 512
        mk.waitt(nc.gpsimd, mm_done.get(qc - 2))
        for hp in range(PT):
            slab_tok[qc % 2] = mk.dma(nc.gpsimd, oslab[:, so + hp * 512:so + (hp + 1) * 512],
                                      osd[hp, qc])
        for mc in range(PT):
            wo = 0
            mk.waitt(nc.gpsimd, wt_done.get((qc, mc - 1)))
            wt = mk.dma(nc.gpsimd, pwbuf[:, wo:wo + 1024], projWt[mc])
            bank = ("pC", "pD")[mc % 2]
            pb = mk.ps_bank(bank)
            mk.waitt(nc.tensor, slab_tok[qc % 2])
            mk.waitt(nc.tensor, wt)
            mk.bw(bank, lambda pb=pb, so=so, wo=wo: mk.mm(
                pb[:, 0:512],
                [(pwbuf[:, wo + hp * 128:wo + (hp + 1) * 128],
                  oslab[:, so + hp * 512:so + (hp + 1) * 512]) for hp in range(PT)]))
            ytmp = mk.scra(8192 + (mc % 2) * 512, 512)

            def pevac(pb=pb, mc=mc, qc=qc, ytmp=ytmp):
                mk.waitt(nc.scalar, yt_done.get((qc * PT + mc) - 2))
                return nc.scalar.activation(out=ytmp, in_=pb[:, 0:512], func=AF.Identity,
                                            bias=pbias[:, mc:mc + 1])
            mk.br(bank, "scalar", pevac)
            at = ("act", mk.cnt["act"])
            mk.waitt(nc.vector, at)
            yt_done[qc * PT + mc] = mk.mark(nc.vector.tensor_tensor(
                ybuf[:, mc * S + qc * 512:mc * S + (qc + 1) * 512], ytmp,
                h1[:, mc * S + qc * 512:mc * S + (qc + 1) * 512], ALU.add), "vector")
            wt_done[(qc, mc)] = mk.banks[bank]["w"]
            mm_done[qc] = mk.banks[bank]["w"]
    mk.barrier()


def emit_xmid(mk: MK, li, slot, x_tiles, ybuf):
    nc = mk.nc
    rsbc = mk.bufs["rsbc_cur"]
    mrsbc = mk.bufs["mrsbc_cur"]
    lng = mk.bufs["w_lng"]
    lnb = mk.bufs["w_lnb"]
    G = mk.sb("xm_G", [128, 8 * PT], F32)
    v = mk.scra(0, S)
    w = mk.scra(2048, S)
    c1 = mk.scra(4096, S)
    t3 = mk.scra(6144, S)
    for p in range(PT):
        nc.vector.tensor_tensor(G[:, p * 8:p * 8 + 1], pp(mk, li, 2, p, slot),
                                lng[:, p:p + 1], ALU.mult)
    t0 = mk.mark(nc.vector.nop(), "vector")
    mk.waitt(nc.vector, t0)
    for p in range(PT):
        nc.vector.tensor_scalar_mul(G[:, p * 8 + 1:p * 8 + 2], G[:, p * 8:p * 8 + 1], -1.0)
    for p in range(PT):
        nc.vector.tensor_tensor(G[:, p * 8 + 2:p * 8 + 3], pp(mk, li, 2, p, slot),
                                lnb[:, p:p + 1], ALU.mult)
    vt = mk.mark(nc.vector.nop(), "vector")
    mk.waitt(nc.vector, vt)
    mk.waitt(nc.scalar, vt)
    for p in range(PT):
        vt = mk.mark(nc.vector.tensor_tensor(v, ybuf[:, p * S:(p + 1) * S], rsbc[:], ALU.mult),
                     "vector")
        mk.waitt(nc.scalar, vt)
        nc.scalar.activation(out=w, in_=v, func=AF.Identity, scale=G[:, p * 8:p * 8 + 1])
        at = mk.mark(nc.scalar.activation(out=c1, in_=mrsbc[:], func=AF.Identity,
                                          scale=G[:, p * 8 + 1:p * 8 + 2],
                                          bias=G[:, p * 8 + 2:p * 8 + 3]), "scalar")
        mk.waitt(nc.vector, at)
        nc.vector.tensor_tensor(t3, w, c1, ALU.add)
        mk.mark(nc.vector.tensor_tensor(x_tiles[p], x_tiles[p], t3, ALU.add), "vector")
    mk.barrier()


def emit_router(mk: MK, slot, h2_tiles, own):
    nc = mk.nc
    wr = mk.bufs["w_router"]
    sqall = mk.bufs["h1"]
    rsqc = mk.sb("rt_rsq", [128, NT], F32)
    zl = mk.sb("rt_zl", [128, NT * 8], F32)
    m8 = mk.sb("rt_m8", [128, NT * 8], F32)
    sel = mk.sb("rt_sel", [128, NT * 8], F32)
    counts = mk.sb("rt_counts", [8, NB], F32)
    offs = mk.sb("rt_offs", [8, 1], F32)
    ones1b = mk.bufs["c_ones1b"]
    sq_toks = []
    for p in range(PT):
        sq_toks.append(mk.mark(nc.scalar.activation(
            out=sqall[:, p * S:(p + 1) * S], in_=h2_tiles[p], func=AF.Square), "scalar"))
    pb = mk.ps_bank("pF")
    mk.waitt(nc.tensor, sq_toks[-1])

    def qgrp():
        last = None
        for t in range(NT):
            for p in range(PT):
                last = nc.tensor.matmul(pb[:, t:t + 1],
                                        sqall[:, p * S + t * 128:p * S + (t + 1) * 128],
                                        ones1b[:], start=(p == 0), stop=(p == PT - 1))
        return last
    mk.bw("pF", qgrp)
    def rsq_read():
        t = mk.mark(nc.scalar.activation(out=rsqc[:], in_=pb[:, 0:NT], func=AF.Ln,
                                         scale=1.0 / D, bias=mk.bufs["c_epsrms"][:]),
                    "scalar")
        mk.waitt(nc.scalar, t)
        return nc.scalar.activation(out=rsqc[:], in_=rsqc[:], func=AF.Exp, scale=-0.5)
    mk.br("pF", "scalar", rsq_read)
    for t in range(NT):
        bank = ("pG", "pH")[t % 2]
        pg = mk.ps_bank(bank)
        mk.bw(bank, lambda pg=pg, t=t: mk.mm(
            pg[:, 0:8], [(h2_tiles[p][:, t * 128:(t + 1) * 128], wr[:, p * 8:(p + 1) * 8])
                         for p in range(PT)]))
        mk.br(bank, "scalar", lambda pg=pg, t=t: nc.scalar.activation(
            out=zl[:, t * 8:(t + 1) * 8], in_=pg[:, 0:8], func=AF.Identity,
            scale=rsqc[:, t:t + 1]))
    mk.waitt(nc.vector, ("act", mk.cnt["act"]))
    for t in range(NT):
        nc.vector.max(out=m8[:, t * 8:(t + 1) * 8], in_=zl[:, t * 8:(t + 1) * 8])
    vt = mk.mark(nc.vector.nop(), "vector")
    for t in range(NT):
        nc.vector.tensor_tensor(sel[:, t * 8:(t + 1) * 8], zl[:, t * 8:(t + 1) * 8],
                                m8[:, t * 8 + 1:t * 8 + 2].to_broadcast([128, 8]), ALU.is_ge)
    vt = mk.mark(nc.vector.nop(), "vector")
    pb2 = mk.ps_bank("pF")
    mk.waitt(nc.tensor, vt)
    mk.bw("pF", lambda: mk.mm(pb2[0:8, 0:1],
                              [(sel[:, t * 8:(t + 1) * 8], mk.bufs["c_ones1f"][:])
                               for t in range(NT)]))
    mk.br("pF", "vector", lambda: nc.vector.tensor_copy(counts[:, slot:slot + 1], pb2[0:8, 0:1]))
    maskpc = mk.bufs["w_mask"]
    mk.waitt(nc.vector, ("vec", mk.cnt["vec"]))
    otmp = mk.sb("rt_otmp", [8, 1], F32)
    t0 = mk.mark(nc.vector.tensor_tensor(otmp[:], counts[:, slot:slot + 1],
                                         maskpc[:, slot:slot + 1], ALU.mult), "vector")
    mk.waitt(nc.vector, t0)
    mk.mark(nc.vector.tensor_tensor(offs[:], offs[:], otmp[:], ALU.add), "vector")
    mk.barrier()


def emit_weight_loads_prefix(mk: MK):
    nc = mk.nc
    pbias = mk.sb("w_projb", [128, PT], F32)
    mk.dma(nc.sync, pbias[:], mk.din("proj_bT", mk.host["proj_bT"])[:])
    lng = mk.sb("w_lng", [128, PT], F32)
    lnb = mk.sb("w_lnb", [128, PT], F32)
    mk.dma(nc.sync, lng[:], mk.din("ln_gT", mk.host["ln_gT"])[:])
    mk.dma(nc.sync, lnb[:], mk.din("ln_bT", mk.host["ln_bT"])[:])
    wr = mk.sb("w_router", [128, 8 * 8], BF16)
    rt = mk.din("router_wt", mk.host["router_wt"])
    for p in range(PT):
        mk.dma(nc.gpsimd, wr[:, p * 8:(p + 1) * 8], rt[p * 128:(p + 1) * 128, :])
    msk = mk.sb("w_mask", [8, 8], F32)
    mk.dma(nc.sync, msk[:], mk.din("mask_before", percore=mk.host_percore["mask_before"])[:])
    nc.vector.memset(mk.sb("rt_offs", [8, 1], F32)[:], 0.0)
    mk.barrier()


def emit_prefix_batch(mk: MK, slot, own):
    nc = mk.nc
    x_all = mk.din("x_all_fm", percore=mk.host_percore["x_all_fm"])
    xr = mk.sb("xres", [128, PT * S], F32)
    if own:
        for p in range(PT):
            mk.dma(nc.sync, xr[:, p * S:(p + 1) * S], x_all[slot, p * 128:(p + 1) * 128, :])
        x_tiles = [xr[:, p * S:(p + 1) * S] for p in range(PT)]
    else:
        xfb = xr[:].bitcast(BF16)
        for p in range(PT):
            mk.dma(nc.gpsimd, xfb[:, p * S:(p + 1) * S], x_all[slot, p * 128:(p + 1) * 128, :])
        x_tiles = [xfb[:, p * S:(p + 1) * S] for p in range(PT)]
    mk.barrier()

    h1 = mk.sb("h1", [128, PT * S], BF16)
    emit_stats(mk, x_tiles, LN_EPS, "cur")
    emit_modln(mk, x_tiles, "cur", 0, 1, 0, slot, h1)
    mk.tap(f"h1_s{slot}", h1[:, 0:S])
    mk.tap(f"h1all_s{slot}", h1[:])
    mk.tap(f"xt1_s{slot}", x_tiles[1])
    mk.tap(f"xt4_s{slot}", x_tiles[4])
    mk.tap(f"xt7_s{slot}", x_tiles[7])

    mk.barrier()
    emit_va_ones(mk)
    mk._eb_zeroed = False
    osd = mk.dscr("osd", [PT, NQC, 128, 512], BF16)
    h1_tiles = [h1[:, p * S:(p + 1) * S] for p in range(PT)]
    wq_sb = mk.sb("w_qkv", [128, 3 * 1024], BF16)
    Wqkvt = mk.din("Wqkvt", mk.host["Wqkvt"])
    for hp in range(PT):
        t1 = mk.dma(nc.gpsimd, wq_sb[:, 0:1024], Wqkvt[hp])
        t2 = mk.dma(nc.gpsimd, wq_sb[:, 1024:2048], Wqkvt[8 + hp])
        t3 = mk.dma(nc.gpsimd, wq_sb[:, 2048:3072], Wqkvt[16 + hp])
        emit_qkv_headpair(mk, hp, h1, wq_sb[:, 0:3 * 1024], t3)
        if hp == 0:
            mk.tap(f"qaraw_s{slot}", mk.scra(8960, S))
        emit_chead(mk, hp)
        for par in range(2):
            emit_eb(mk, 2 * hp + par, par)
        if hp == 0:
            mk.tap(f"negc_s{slot}", mk.bufs["at_negc"][:])
            mk.tap(f"qa_s{slot}", mk.scra(8960, S))
            mk.tap(f"ka_s{slot}", mk.scra(11008, S))
            mk.tap(f"eb_s{slot}", mk.scra(4096, 4864))
            mk.tap(f"va_s{slot}", mk.scra(0, 4096))
        mk.barrier()
        emit_attn_pair(mk, hp, osd)

    ybuf = mk.sb("hbuf", [128, PT * S], BF16)
    emit_proj_y(mk, slot, osd, h1, ybuf)
    mk.tap(f"yb_s{slot}", ybuf[:, 0:S])
    emit_stats(mk, [ybuf[:, p * S:(p + 1) * S] for p in range(PT)], LN_EPS, "cur")
    mk.tap(f"yrows_s{slot}", mk.scraf(512, 3 * S)[0:1, :])
    emit_xmid(mk, 0, slot, x_tiles, ybuf)
    mk.tap(f"x1_s{slot}", x_tiles[0])
    mk.tap(f"xmG_s{slot}", mk.bufs["xm_G"][:])
    mk.tap(f"modT0_s{slot}", mk.bufs["modT0"][:])
    mk.tap(f"rsy_s{slot}", mk.bufs["rsbc_cur"][:, 0:S])
    mk.tap(f"mrsy_s{slot}", mk.bufs["mrsbc_cur"][:, 0:S])

    h2 = mk.sb("hbuf", [128, PT * S], BF16)
    emit_stats(mk, x_tiles, LN_EPS, "cur")
    emit_modln(mk, x_tiles, "cur", 0, 4, 3, slot, h2)
    mk.tap(f"h2_s{slot}", h2[:, 0:S])
    emit_router(mk, slot, [h2[:, p * S:(p + 1) * S] for p in range(PT)], own)
    return x_tiles, h2



def emit_moe_ranks(mk: MK, slot):
    nc = mk.nc
    sel = mk.bufs["rt_sel"]
    zl = mk.bufs["rt_zl"]
    m8 = mk.bufs["rt_m8"]
    offs = mk.bufs["rt_offs"]
    identf = mk.bufs["c_identf"]
    selT = mk.scraf(0, 2048)[0:8, :]
    zer = mk.scraf(2048, 2048)[0:8, :]
    csum = mk.scraf(4096, 2048)[0:8, :]
    rank = mk.scraf(2048, 2048)[0:8, :]
    grank = mk.scraf(4096, 2048)[0:8, :]
    validT = mk.scraf(6144, 2048)[0:8, :]
    for t in range(NT):
        pf = mk.ps_bank("pF")
        mk.bw("pF", lambda t=t: nc.tensor.transpose(pf[0:8, 0:128], sel[:, t * 8:(t + 1) * 8],
                                                    identf[:]))
        mk.br("pF", "scalar", lambda t=t: nc.scalar.copy(out=selT[:, t * 128:(t + 1) * 128],
                                                         in_=pf[0:8, 0:128]))
    vt = mk.mark(nc.vector.memset(zer, 0.0), "vector")
    mk.barrier()
    vt = mk.mark(nc.vector.tensor_tensor_scan(csum, selT, zer, 0.0, ALU.add, ALU.add),
                 "vector")
    mk.waitt(nc.vector, vt)
    vt = mk.mark(nc.vector.tensor_tensor(rank, csum, selT, ALU.subtract), "vector")
    mk.waitt(nc.scalar, vt)
    at = mk.mark(nc.scalar.activation(out=grank, in_=rank, func=AF.Identity,
                                      bias=offs[:, 0:1]), "scalar")
    mk.waitt(nc.vector, at)
    vt = mk.mark(nc.vector.tensor_scalar(validT, grank, float(CAP), None, ALU.is_lt),
                 "vector")
    mk.waitt(nc.vector, vt)
    vt = mk.mark(nc.vector.tensor_tensor(validT, validT, selT, ALU.mult), "vector")
    mk.barrier()
    valid_tm = mk.scraf(8192, 128)
    gall = mk.scraf(8320, 128)
    abuf = mk.scraf(8448, 128)
    bbuf = mk.scraf(8576, 128)
    zd = mk.scraf(8704, 128)
    w1b = mk.scraf(8832, 128)
    w2b = mk.scraf(8960, 128)
    for t in range(NT):
        pf = mk.ps_bank("pF")
        mk.bw("pF", lambda t=t: nc.tensor.transpose(
            pf[0:128, 0:8], validT[:, t * 128:(t + 1) * 128], identf[0:8, 0:8]))
        mk.br("pF", "scalar", lambda t=t: nc.scalar.copy(
            out=valid_tm[:, t * 8:(t + 1) * 8], in_=pf[0:128, 0:8]))
    for t in range(NT):
        nc.vector.tensor_tensor(abuf[:, t * 8:(t + 1) * 8], zl[:, t * 8:(t + 1) * 8],
                                m8[:, t * 8:t * 8 + 1].to_broadcast([128, 8]), ALU.is_ge)
        nc.vector.tensor_tensor(zd[:, t * 8:t * 8 + 1], m8[:, t * 8:t * 8 + 1],
                                m8[:, t * 8 + 1:t * 8 + 2], ALU.subtract)
    vt = mk.mark(nc.vector.nop(), "vector")
    mk.waitt(nc.vector, vt)
    vt = mk.mark(nc.vector.tensor_tensor(bbuf, sel[:], abuf, ALU.subtract), "vector")
    mk.waitt(nc.scalar, vt)
    at = mk.mark(nc.scalar.activation(out=w1b, in_=zd, func=AF.Sigmoid), "scalar")
    mk.waitt(nc.vector, at)
    t0 = mk.mark(nc.vector.tensor_scalar(w2b, w1b, -1.0, None, ALU.mult), "vector")
    mk.waitt(nc.vector, t0)
    t0 = mk.mark(nc.vector.tensor_scalar(w2b, w2b, 1.0, None, ALU.add), "vector")
    mk.waitt(nc.vector, t0)
    for t in range(NT):
        nc.vector.tensor_tensor(abuf[:, t * 8:(t + 1) * 8], abuf[:, t * 8:(t + 1) * 8],
                                w1b[:, t * 8:t * 8 + 1].to_broadcast([128, 8]), ALU.mult)
        nc.vector.tensor_tensor(bbuf[:, t * 8:(t + 1) * 8], bbuf[:, t * 8:(t + 1) * 8],
                                w2b[:, t * 8:t * 8 + 1].to_broadcast([128, 8]), ALU.mult)
    vt = mk.mark(nc.vector.nop(), "vector")
    mk.waitt(nc.vector, vt)
    t0 = mk.mark(nc.vector.tensor_tensor(gall, abuf, bbuf, ALU.add), "vector")
    mk.waitt(nc.vector, t0)
    vt = mk.mark(nc.vector.tensor_tensor(gall, gall, valid_tm, ALU.mult), "vector")
    mk.barrier()
    gT = mk.scraf(0, 2048)[0:8, :]
    for t in range(NT):
        pf = mk.ps_bank("pF")
        mk.bw("pF", lambda t=t: nc.tensor.transpose(pf[0:8, 0:128],
                                                    gall[:, t * 8:(t + 1) * 8], identf[:]))
        mk.br("pF", "scalar", lambda t=t: nc.scalar.copy(out=gT[:, t * 128:(t + 1) * 128],
                                                         in_=pf[0:8, 0:128]))
    mk.barrier()
    gTd = mk.dscr("gTd", [8, 2048], F32)
    mk.dma(nc.sync, gTd[:], gT)
    mk.barrier()
    if mk.enable_taps:
        mk.tap("moe_gT", gT)
        mk.tap("moe_validT", validT)
    return gTd


def emit_moe_dense(mk: MK, slot, h2):
    nc = mk.nc
    gTd = emit_moe_ranks(mk, slot)
    rgt = mk.din("rgt", mk.host["rgt"])
    rut = mk.din("rut", mk.host["rut"])
    rdt = mk.din("rdt", mk.host["rdt"])
    rtfm = mk.dscr("rtfm", [PT, 128, S], BF16)
    wgu = mk.bufs["at_ebst"][:].bitcast(BF16)
    mk._wgu = wgu
    wdn = mk.bufs["w_qkv"]
    h1b = mk.bufs["h1"]
    harena = mk.scra(0, 16384)

    def hst(mc, half):
        base = mc * 1024 + half * 512
        if mc < 16:
            return h1b[:, base:base + 512]
        return harena[:, (mc - 16) * 1024 + half * 512:(mc - 16) * 1024 + half * 512 + 512]

    sstg = mk.scra(16384, 1024)
    gbc = mk.scra(17408, 1024)
    for tc in range(2):
        for e in range(NE):
            mk.barrier()
            grow = mk.scraf(0, 1024)[0:1, :]
            mk.dma(nc.sync, grow, gTd[e:e + 1, tc * 1024:(tc + 1) * 1024])
            mk.barrier()
            pg = mk.ps_bank("pG")
            for j in range(2):
                mk.bw("pG", lambda j=j: mk.mm(
                    pg[:, 0:512],
                    [(mk.bufs["c_onesr"][:], grow[:, j * 512:(j + 1) * 512])]))
                mk.br("pG", "scalar", lambda j=j: nc.scalar.copy(
                    out=gbc[:, j * 512:(j + 1) * 512], in_=pg[:, 0:512]))
            mk.barrier()
            ptoks = {}
            for mc in range(32):
                mk.waitt(nc.gpsimd, ptoks.get(("gmm", mc - 1)))
                tw = mk.dma(nc.gpsimd, wgu[:, 0:1024], rgt[e, mc])
                mk.waitt(nc.tensor, tw)
                mk.waitt(nc.tensor, ptoks.get(("sil", mc - 1)))
                for half, bank in ((0, "pA"), (1, "pB")):
                    pb = mk.ps_bank(bank)
                    mk.bw(bank, lambda pb=pb, half=half: mk.mm(
                        pb[:, 0:512],
                        [(wgu[:, p * 128:(p + 1) * 128],
                          h2[:, p * S + tc * 1024 + half * 512:p * S + tc * 1024 + half * 512 + 512])
                         for p in range(PT)]))
                ptoks[("gmm", mc)] = mk.banks["pB"]["w"]
                mk.waitt(nc.gpsimd, ptoks.get(("umm", mc - 1)))
                tw = mk.dma(nc.gpsimd, wgu[:, 1024:2048], rut[e, mc])
                mk.waitt(nc.tensor, tw)
                for half, bank in ((0, "pC"), (1, "pD")):
                    pb = mk.ps_bank(bank)
                    mk.bw(bank, lambda pb=pb, half=half: mk.mm(
                        pb[:, 0:512],
                        [(wgu[:, 1024 + p * 128:1024 + (p + 1) * 128],
                          h2[:, p * S + tc * 1024 + half * 512:p * S + tc * 1024 + half * 512 + 512])
                         for p in range(PT)]))
                last_sil = None
                for half in range(2):
                    gb_ = ("pA", "pB")[half]
                    ub_ = ("pC", "pD")[half]
                    pgp = mk.ps_bank(gb_)
                    pup = mk.ps_bank(ub_)
                    mk.waitt(nc.scalar, mk.banks[gb_]["w"])
                    at = mk.mark(nc.scalar.activation(out=sstg[:, half * 512:(half + 1) * 512],
                                                      in_=pgp[:, 0:512], func=AF.Silu), "scalar")
                    mk.banks[gb_]["r"] = at
                    mk.waitt(nc.vector, at)
                    mk.waitt(nc.vector, mk.banks[ub_]["w"])
                    vt = mk.mark(nc.vector.tensor_tensor(
                        hst(mc, half), sstg[:, half * 512:(half + 1) * 512],
                        pup[:, 0:512], ALU.mult), "vector")
                    mk.banks[ub_]["r"] = vt
                    last_sil = vt
                ptoks[("sil", mc)] = last_sil
                ptoks[("umm", mc)] = mk.banks["pD"]["w"]
            mk.barrier()
            for fm in range(PT):
                for kh in range(2):
                    tw = mk.dma(nc.gpsimd, wdn[:, 0:2048],
                                rdt[e, fm][:, kh * 2048:(kh + 1) * 2048])
                    mk.waitt(nc.tensor, tw)
                    for half, bank in ((0, "pE"), (1, "pF")):
                        pb = mk.ps_bank(bank)
                        if kh == 0:
                            mk.waitt(nc.tensor, mk.banks[bank]["r"])

                        def grp(pb=pb, half=half, kh=kh):
                            last = None
                            for k in range(16):
                                last = nc.tensor.matmul(
                                    pb[:, 0:512], wdn[:, k * 128:(k + 1) * 128],
                                    hst(kh * 16 + k, half),
                                    start=(kh == 0 and k == 0), stop=(kh == 1 and k == 15))
                            return last
                        tok = mk.mark(grp(), "tensor")
                        mk.banks[bank]["w"] = tok
                    if kh == 0:
                        mk.barrier()
                for half, bank in ((0, "pE"), (1, "pF")):
                    pb = mk.ps_bank(bank)
                    mk.waitt(nc.vector, mk.banks[bank]["w"])
                    hsl = slice(tc * 1024 + half * 512, tc * 1024 + half * 512 + 512)
                    tmpb = mk.scra(16384, 512)
                    rchunk = mk.scra(16896, 512)
                    if e > 0:
                        tr = mk.dma(nc.sync, rchunk, rtfm[fm][:, hsl])
                        mk.waitt(nc.vector, tr)
                    vt = mk.mark(nc.vector.tensor_tensor(
                        tmpb, pb[:, 0:512], gbc[:, half * 512:(half + 1) * 512], ALU.mult),
                        "vector")
                    mk.banks[bank]["r"] = vt
                    mk.waitt(nc.vector, vt)
                    if e > 0:
                        vt = mk.mark(nc.vector.tensor_tensor(tmpb, tmpb, rchunk, ALU.add),
                                     "vector")
                        mk.waitt(nc.vector, vt)
                    mk.waitt(nc.sync, vt)
                    mk.dma(nc.sync, rtfm[fm][:, hsl], tmpb)
                    mk.barrier()
            mk.barrier()
        mk.barrier()


def emit_shared(mk: MK, h2):
    nc = mk.nc
    sgt = mk.din("sgt", mk.host["sgt"])
    sut = mk.din("sut", mk.host["sut"])
    sdt = mk.din("sdt", mk.host["sdt"])
    shfm = mk.dscr("shfm", [PT, 128, S], BF16)
    wgu = mk._wgu
    h1b = mk.bufs["h1"]
    harena = mk.scra(0, 16384)
    sstg = mk.scra(16384, 512)

    def hstS(k):
        if k < 32:
            return h1b[:, k * 512:(k + 1) * 512]
        return harena[:, (k - 32) * 512:(k - 31) * 512]

    for tc in range(4):
        mk.barrier()
        ptoks = {}
        for k in range(IS // 128):
            mk.waitt(nc.gpsimd, ptoks.get(("g", k - 1)))
            tw = mk.dma(nc.gpsimd, wgu[:, 0:1024], sgt[k])
            mk.waitt(nc.tensor, tw)
            mk.waitt(nc.tensor, ptoks.get(k - 1))
            pa = mk.ps_bank("pA")
            mk.bw("pA", lambda: mk.mm(
                pa[:, 0:512],
                [(wgu[:, p * 128:(p + 1) * 128], h2[:, p * S + tc * 512:p * S + (tc + 1) * 512])
                 for p in range(PT)]))
            ptoks[("g", k)] = mk.banks["pA"]["w"]
            mk.waitt(nc.gpsimd, ptoks.get(("u", k - 1)))
            tw = mk.dma(nc.gpsimd, wgu[:, 1024:2048], sut[k])
            mk.waitt(nc.tensor, tw)
            pb = mk.ps_bank("pB")
            mk.bw("pB", lambda: mk.mm(
                pb[:, 0:512],
                [(wgu[:, 1024 + p * 128:1024 + (p + 1) * 128],
                  h2[:, p * S + tc * 512:p * S + (tc + 1) * 512]) for p in range(PT)]))
            mk.waitt(nc.scalar, mk.banks["pA"]["w"])
            at = mk.mark(nc.scalar.activation(out=sstg[:], in_=pa[:, 0:512], func=AF.Silu),
                         "scalar")
            mk.banks["pA"]["r"] = at
            mk.waitt(nc.vector, at)
            mk.waitt(nc.vector, mk.banks["pB"]["w"])
            vt = mk.mark(nc.vector.tensor_tensor(hstS(k), sstg[:], pb[:, 0:512], ALU.mult),
                         "vector")
            mk.banks["pB"]["r"] = vt
            ptoks[k] = vt
            ptoks[("u", k)] = mk.banks["pB"]["w"]
        mk.barrier()
        for fm in range(PT):
            pe = mk.ps_bank("pE")
            for q in range(4):
                tw = mk.dma(nc.gpsimd, wgu[:, 0:2048],
                            sdt[fm][:, q * 2048:(q + 1) * 2048])
                mk.waitt(nc.tensor, tw)
                if q == 0:
                    mk.waitt(nc.tensor, mk.banks["pE"]["r"])
                last = None
                for k in range(16):
                    last = nc.tensor.matmul(pe[:, 0:512], wgu[:, k * 128:(k + 1) * 128],
                                            hstS(q * 16 + k),
                                            start=(q == 0 and k == 0),
                                            stop=(q == 3 and k == 15))
                tok = mk.mark(last, "tensor")
                mk.banks["pE"]["w"] = tok
                mk.barrier()
            mk.waitt(nc.scalar, mk.banks["pE"]["w"])
            at = mk.mark(nc.scalar.copy(out=sstg[:], in_=pe[:, 0:512]), "scalar")
            mk.banks["pE"]["r"] = at
            mk.waitt(nc.sync, at)
            mk.dma(nc.sync, shfm[fm][:, tc * 512:(tc + 1) * 512], sstg[:])
            mk.barrier()


def emit_x2(mk: MK, slot, x_tiles):
    nc = mk.nc
    rtfm = mk.dram["rtfm"]
    shfm = mk.dram["shfm"]
    for p in range(PT):
        rsl = mk.scra(0, 2048)
        ssl = mk.scra(2048, 2048)
        sm = mk.scra(4096, 2048)
        w = mk.scraf(3072, 2048)
        t1 = mk.dma(nc.sync, rsl[:], rtfm[p])
        t2 = mk.dma(nc.sync, ssl[:], shfm[p])
        mk.waitt(nc.vector, t1)
        mk.waitt(nc.vector, t2)
        vt = mk.mark(nc.vector.tensor_tensor(sm, rsl, ssl, ALU.add), "vector")
        mk.waitt(nc.scalar, vt)
        at = mk.mark(nc.scalar.activation(out=w, in_=sm, func=AF.Identity,
                                          scale=pp(mk, 0, 5, p, slot)), "scalar")
        mk.waitt(nc.vector, at)
        mk.mark(nc.vector.tensor_tensor(x_tiles[p], x_tiles[p], w, ALU.add), "vector")
        mk.barrier()
    if mk.enable_taps:
        mk.tap("x2t0", x_tiles[0])



def emit_ssm_kernel(mk: MK):
    nc = mk.nc
    Asb = mk.sb("ss_A", [16, 16], F32)
    mk.dma(nc.sync, Asb[:], mk.din("A", mk.host["A"])[:])
    Pa = mk.sb("ss_Pa", [16, 2 * 16], F32)
    Pb = mk.sb("ss_Pb", [16, 2 * 16], F32)
    RT = mk.scraf(0, 2048)[0:16, :]
    Cpc = mk.sb("ss_Cp", [16, 1], F32)
    Bpc = mk.sb("ss_Bp", [16, 1], F32)
    mk.dma(nc.sync, Cpc[:], mk.din("Cp_col", mk.host["Cp_col"])[:])
    mk.dma(nc.sync, Bpc[:], mk.din("Bp_col", mk.host["Bp_col"])[:])
    mk.barrier()
    at = mk.mark(nc.scalar.activation(out=Pa[:, 0:16], in_=Asb[:], func=AF.Tanh), "scalar")
    mk.waitt(nc.tensor, at)
    pf = mk.ps_bank("pF")
    mk.bw("pF", lambda: nc.tensor.transpose(pf[0:16, 0:16], Pa[:, 0:16],
                                            mk.bufs["c_identf"][0:16, 0:16]))
    mk.br("pF", "scalar", lambda: nc.scalar.copy(out=Pa[:, 16:32], in_=pf[0:16, 0:16]))
    sc = mk.mark(nc.scalar.copy(out=RT[:, 0:1], in_=Cpc[:]), "scalar")
    mk.barrier()
    Pah = mk.sb("ss_Pah", [16, 32], BF16)
    Pal = mk.sb("ss_Pal", [16, 32], BF16)
    Pbh = mk.sb("ss_Pbh", [16, 32], BF16)
    Pbl = mk.sb("ss_Pbl", [16, 32], BF16)
    Bph = mk.sb("ss_Bph", [16, 1], BF16)
    Bpl = mk.sb("ss_Bpl", [16, 1], BF16)
    RTh = mk.scra(8192, 2048)[0:16, :]
    RTl = mk.scra(10240, 2048)[0:16, :]
    hif = mk.scraf(6144, 512)[0:16, :]

    def hilo(fap, hap, lap, cols):
        mk.waitt(nc.scalar, getattr(mk, "_hilo_last", None))
        a1 = mk.mark(nc.scalar.activation(out=hap, in_=fap, func=AF.Identity), "scalar")
        mk.waitt(nc.scalar, a1)
        a2 = mk.mark(nc.scalar.activation(out=hif[:, 0:cols], in_=hap, func=AF.Identity),
                     "scalar")
        mk.waitt(nc.vector, a2)
        t = mk.mark(nc.vector.tensor_tensor(lap, fap, hif[:, 0:cols], ALU.subtract),
                    "vector")
        mk.waitt(nc.tensor, t)
        mk._hilo_last = t
        return t

    def mm3(pap, lh, ll, rh, rl):
        nc.tensor.matmul(pap, lh, rh, start=True, stop=False)
        nc.tensor.matmul(pap, ll, rh, start=False, stop=False)
        return nc.tensor.matmul(pap, lh, rl, start=False, stop=True)

    hilo(Pa[:, 0:32], Pah[:], Pal[:], 32)
    hilo(RT[:, 0:1], RTh[:, 0:1], RTl[:, 0:1], 1)
    hilo(Bpc[:], Bph[:], Bpl[:], 1)
    mk.barrier()
    m = 1
    cur, nxt = Pa, Pb
    curh, curl, nxth, nxtl = Pah, Pal, Pbh, Pbl
    while m < S:
        for off in range(0, m, 512):
            cs = min(512, m - off)
            pg = mk.ps_bank("pG")
            mk.bw("pG", lambda off=off, cs=cs, curh=curh, curl=curl: mm3(
                pg[0:16, 0:cs], curh[:, 0:16], curl[:, 0:16],
                RTh[:, off:off + cs], RTl[:, off:off + cs]))
            mk.br("pG", "scalar", lambda off=off, cs=cs: nc.scalar.copy(
                out=RT[:, m + off:m + off + cs], in_=pg[0:16, 0:cs]))
            mk.barrier()
            hilo(RT[:, m + off:m + off + cs], RTh[:, m + off:m + off + cs],
                 RTl[:, m + off:m + off + cs], cs)
        if 2 * m < S:
            ph = mk.ps_bank("pH")
            mk.bw("pH", lambda curh=curh, curl=curl: mm3(
                ph[0:16, 0:16], curh[:, 16:32], curl[:, 16:32],
                curh[:, 0:16], curl[:, 0:16]))
            mk.br("pH", "scalar", lambda nxt=nxt: nc.scalar.copy(out=nxt[:, 0:16],
                                                                 in_=ph[0:16, 0:16]))
            ph2 = mk.ps_bank("pH")
            mk.bw("pH", lambda curh=curh, curl=curl: mm3(
                ph2[0:16, 0:16], curh[:, 0:16], curl[:, 0:16],
                curh[:, 16:32], curl[:, 16:32]))
            mk.br("pH", "scalar", lambda nxt=nxt: nc.scalar.copy(out=nxt[:, 16:32],
                                                                 in_=ph2[0:16, 0:16]))
            mk.barrier()
            hilo(nxt[:, 0:32], nxth[:], nxtl[:], 32)
        mk.barrier()
        cur, nxt = nxt, cur
        curh, curl, nxth, nxtl = nxth, nxtl, curh, curl
        m *= 2
    BpM = mk.sb("ss_BpM", [16, 16], F32)
    vt = mk.mark(nc.vector.tensor_copy(BpM[:], Bpc[:].to_broadcast([16, 16])), "vector")
    mk.waitt(nc.scalar, vt)
    BpMh = mk.sb("ss_BpMh", [16, 16], BF16)
    BpMl = mk.sb("ss_BpMl", [16, 16], BF16)
    hilo(BpM[:], BpMh[:], BpMl[:], 16)
    mk.barrier()
    kdrow = mk.scraf(2048, 2048)[0:1, :]
    for off in range(0, S, 512):
        pg = mk.ps_bank("pG")
        mk.bw("pG", lambda off=off: mm3(pg[0:16, 0:512], BpMh[:], BpMl[:],
                                        RTh[:, off:off + 512], RTl[:, off:off + 512]))
        mk.br("pG", "scalar", lambda off=off: nc.scalar.copy(out=kdrow[:, off:off + 512],
                                                             in_=pg[0:1, 0:512]))
    kpad = mk.dscr("kpad", [1, 4224], F32)
    zrow = mk.scraf(4096, 2048)[0:1, :]
    vt = mk.mark(nc.vector.memset(zrow, 0.0), "vector")
    mk.barrier()
    mk.dma(nc.sync, kpad[:, 0:2047], zrow[:, 0:2047])
    mk.dma(nc.sync, kpad[:, 2047:4095], kdrow[:])
    mk.dma(nc.sync, kpad[:, 4095:4224], zrow[:, 0:129])
    idx = mk.sb("ss_idx", [128, 1], I32)
    gt = mk.mark(nc.gpsimd.iota(idx[:], [[1, 1]], base=0, channel_multiplier=1), "gpsimd")
    mk.waitt(nc.vector, gt)
    t0 = mk.mark(nc.vector.tensor_scalar(idx[:], idx[:], -1, None, ALU.mult), "vector")
    mk.waitt(nc.vector, t0)
    t0 = mk.mark(nc.vector.tensor_scalar(idx[:], idx[:], 2047, None, ALU.add), "vector")
    mk.barrier()
    tkf = mk.scraf(3072, 2048)
    mk.dmaI(nc.gpsimd, tkf[:], None, kpad[:], IndirectOffsetOnAxis(idx[:, 0:1], 1))
    mk.barrier()
    tkb = mk.scra(12288, 2048)
    at = mk.mark(nc.scalar.activation(out=tkb, in_=tkf, func=AF.Identity), "scalar")
    Dpd = mk.din("Dp", mk.host["Dp"])
    Dps = mk.sb("ss_Dps", [1, 1], F32)
    mk.dma(nc.sync, Dps[:], Dpd[:])
    Dp128 = mk.sb("ss_Dp128", [128, 1], F32)
    mk.barrier()
    pg = mk.ps_bank("pG")
    mk.bw("pG", lambda: nc.tensor.matmul(pg[:, 0:1], mk.bufs["c_onesr"][:], Dps[:],
                                         start=True, stop=True))
    mk.br("pG", "scalar", lambda: nc.scalar.copy(out=Dp128[:], in_=pg[:, 0:1]))
    mk.barrier()
    if mk.enable_taps:
        mk.tap("ss_kd", kdrow)


def emit_l1_mixer(mk: MK, slot, x_tiles):
    nc = mk.nc
    h3 = mk.sb("hbuf", [128, PT * S], BF16)
    emit_stats(mk, x_tiles, LN_EPS, "cur")
    emit_modln(mk, x_tiles, "cur", 1, 1, 0, slot, h3)
    if mk.enable_taps:
        mk.tap("h3t", h3[:, 0:S])
    W_int = mk.din("W_int", mk.host["W_int"])
    b_inF = mk.sb("ss_binF", [128, 16], F32)
    mk.dma(nc.sync, b_inF[:], mk.din("b_inF", mk.host["b_inF"])[:])
    cwF = mk.sb("ss_cwF", [128, 16 * 4], F32)
    cbF = mk.sb("ss_cbF", [128, 16], F32)
    cwd = mk.din("conv_wF", mk.host["conv_wF"])
    cbd = mk.din("conv_bF", mk.host["conv_bF"])
    for d in range(16):
        mk.dma(nc.sync, cwF[:, d * 4:(d + 1) * 4], cwd[d])
        mk.dma(nc.sync, cbF[:, d:d + 1], cbd[d])
    wbuf = mk._wgu
    utmd = mk.dscr("utmd", [S, DI], BF16)
    mk.barrier()
    u0t = mk.scra(0, 2048)
    uc = mk.scra(2048, 2048)
    cstg = mk.scra(4096, 2048)
    tstg = mk.scra(6144, 256)
    for d in range(16):
        tw = mk.dma(nc.gpsimd, wbuf[:, 0:1024], W_int[d])
        mk.waitt(nc.tensor, tw)
        mk.barrier()
        for qc in range(NQC):
            pa = mk.ps_bank("pA")
            mk.bw("pA", lambda qc=qc: mk.mm(
                pa[:, 0:512],
                [(wbuf[:, p * 128:(p + 1) * 128], h3[:, p * S + qc * 512:p * S + (qc + 1) * 512])
                 for p in range(PT)]))
            mk.br("pA", "scalar", lambda qc=qc: nc.scalar.activation(
                out=u0t[:, qc * 512:(qc + 1) * 512], in_=pa[:, 0:512], func=AF.Identity,
                bias=b_inF[:, d:d + 1]))
        mk.barrier()
        at = mk.mark(nc.scalar.activation(out=uc, in_=u0t, func=AF.Identity,
                                          scale=cwF[:, d * 4 + 3:d * 4 + 4]), "scalar")
        for j in range(3):
            sft = 3 - j
            a2 = mk.mark(nc.scalar.activation(out=cstg[:, 0:S - sft], in_=u0t[:, 0:S - sft],
                                              func=AF.Identity,
                                              scale=cwF[:, d * 4 + j:d * 4 + j + 1]), "scalar")
            mk.waitt(nc.vector, a2)
            at = mk.mark(nc.vector.tensor_tensor(uc[:, sft:S], uc[:, sft:S],
                                                 cstg[:, 0:S - sft], ALU.add), "vector")
            mk.waitt(nc.vector, at)
        vt = mk.mark(nc.vector.tensor_tensor(uc, uc, cbF[:, d:d + 1].to_broadcast([128, S]),
                                             ALU.add), "vector")
        mk.barrier()
        for c in range(NT):
            tt = mk.dmaT(nc.scalar, tstg[:, (c % 2) * 128:(c % 2) * 128 + 128],
                         uc[:, c * 128:(c + 1) * 128])
            mk.waitt(nc.sync, tt)
            st = mk.dma(nc.sync, utmd[c * 128:(c + 1) * 128, d * 128:(d + 1) * 128],
                        tstg[:, (c % 2) * 128:(c % 2) * 128 + 128])
            if c >= 1:
                mk.waitt(nc.scalar, st)
        mk.barrier()
    emit_ssm_kernel(mk)
    tkb = mk.scra(12288, 2048)
    Dp128 = mk.bufs["ss_Dp128"]


    ysfmd = mk.dscr("ysfmd", [16, 128, S], BF16)
    urhs = [mk.scra(14336, 2048), mk.scra(16384, 2048)]
    dstg = mk.scraf(4224, 512)
    ysil = mk.scra(6144, 2048)
    tstg2 = mk.scra(8192, 256)
    for c in range(NT):
        banks = ["pA", "pB", "pC", "pD"]
        utoks = {}
        for ci in range(c + 1):
            ub = urhs[ci % 2]
            mk.waitt(nc.sync, utoks.get(ci - 2))
            tu = mk.dma(nc.sync, ub[:], utmd[ci * 128:(ci + 1) * 128, :])
            mk.waitt(nc.tensor, tu)
            lm = None
            for dc in range(4):
                pb = mk.ps_bank(banks[dc])
                if ci == 0:
                    mk.waitt(nc.tensor, mk.banks[banks[dc]]["r"])
                lm = nc.tensor.matmul(pb[:, 0:512],
                                      tkb[:, (c - ci) * 128:(c - ci + 1) * 128],
                                      ub[:, dc * 512:(dc + 1) * 512],
                                      start=(ci == 0), stop=(ci == c))
            utoks[ci] = mk.mark(lm, "tensor")
            if ci == c:
                for dc in range(4):
                    mk.banks[banks[dc]]["w"] = utoks[ci]
        for dc in range(4):
            pb = mk.ps_bank(banks[dc])
            mk.waitt(nc.scalar, mk.banks[banks[dc]]["w"])
            at = mk.mark(nc.scalar.activation(out=dstg[:], in_=urhs[c % 2][:, dc * 512:(dc + 1) * 512],
                                              func=AF.Identity, scale=Dp128[:, 0:1]), "scalar")
            mk.waitt(nc.vector, at)
            vt = mk.mark(nc.vector.tensor_tensor(dstg[:], dstg[:], pb[:, 0:512], ALU.add),
                         "vector")
            mk.banks[banks[dc]]["r"] = vt
            mk.waitt(nc.scalar, vt)
            at = mk.mark(nc.scalar.activation(out=ysil[:, dc * 512:(dc + 1) * 512],
                                              in_=dstg[:], func=AF.Silu), "scalar")
        mk.barrier()
        for d in range(16):
            tt = mk.dmaT(nc.scalar, tstg2[:, (d % 2) * 128:(d % 2) * 128 + 128],
                         ysil[:, d * 128:(d + 1) * 128])
            mk.waitt(nc.sync, tt)
            st = mk.dma(nc.sync, ysfmd[d][:, c * 128:(c + 1) * 128],
                        tstg2[:, (d % 2) * 128:(d % 2) * 128 + 128])
            if d >= 1:
                mk.waitt(nc.scalar, st)
        mk.barrier()
    W_outt = mk.din("W_outt", mk.host["W_outt"])
    b_outT = mk.sb("ss_boT", [128, PT], F32)
    mk.dma(nc.sync, b_outT[:], mk.din("b_outT", mk.host["b_outT"])[:])
    gbo = mk.sb("ss_gbo", [128, 8 * PT], F32)
    mk.barrier()
    for p in range(PT):
        nc.vector.tensor_tensor(gbo[:, p * 8:p * 8 + 1], pp(mk, 1, 2, p, slot),
                                b_outT[:, p:p + 1], ALU.mult)
    mk.barrier()
    ysl = mk.scra(0, 8192)
    wst = mk.scraf(5120, 512)
    wob = mk._wgu
    for qc in range(NQC):
        for k in range(16):
            mk.dma(nc.sync, ysl[:, k * 512:(k + 1) * 512],
                   ysfmd[k][:, qc * 512:(qc + 1) * 512])
        mk.barrier()
        for mc in range(PT):
            tw = mk.dma(nc.gpsimd, wob[:, 0:2048], W_outt[mc])
            mk.waitt(nc.tensor, tw)
            pa = mk.ps_bank("pA")
            mk.bw("pA", lambda mc=mc: mk.mm(
                pa[:, 0:512],
                [(wob[:, k * 128:(k + 1) * 128], ysl[:, k * 512:(k + 1) * 512])
                 for k in range(16)]))
            mk.waitt(nc.scalar, mk.banks["pA"]["w"])
            at = mk.mark(nc.scalar.activation(out=wst, in_=pa[:, 0:512], func=AF.Identity,
                                              scale=pp(mk, 1, 2, mc, slot),
                                              bias=gbo[:, mc * 8:mc * 8 + 1]), "scalar")
            mk.banks["pA"]["r"] = at
            mk.waitt(nc.vector, at)
            mk.mark(nc.vector.tensor_tensor(x_tiles[mc][:, qc * 512:(qc + 1) * 512],
                                            x_tiles[mc][:, qc * 512:(qc + 1) * 512],
                                            wst, ALU.add), "vector")
            mk.barrier()
    if mk.enable_taps:
        mk.tap("x3t0", x_tiles[0])



def _r3(ap, nb, b=16):
    return ap.rearrange("p (a b) -> p a b", b=b)


def emit_qblk(mk: MK, out_ap, in_ap, cols):
    nc = mk.nc
    nb = cols // 16
    s1 = mk.scraf(6144, cols)
    qacc = mk.scraf(7168, cols)
    tq = mk.scraf(8192, cols)
    s2 = mk.sb("q_s2", [128, 64], F32)[:, 0:nb]
    s3 = mk.sb("q_s3", [128, 64], F32)[:, 0:nb]
    at = mk.mark(nc.scalar.activation(out=s1, in_=in_ap, func=AF.Abs), "scalar")
    mk.waitt(nc.vector, at)
    vt = mk.mark(nc.vector.tensor_reduce(_r3(s2, nb, 1), _r3(s1, nb), AX.X, ALU.max),
                 "vector")
    mk.waitt(nc.vector, vt)
    t = mk.mark(nc.vector.tensor_scalar(s2, s2, 1.0 / 6.0, None, ALU.mult), "vector")
    mk.waitt(nc.vector, t)
    t = mk.mark(nc.vector.tensor_scalar(s2, s2, 1e-12, None, ALU.max), "vector")
    mk.waitt(nc.scalar, t)
    at = mk.mark(nc.scalar.activation(out=s3, in_=s2, func=AF.Ln), "scalar")
    mk.waitt(nc.scalar, at)
    at = mk.mark(nc.scalar.activation(out=s3, in_=s3, func=AF.Exp, scale=-1.0), "scalar")
    mk.waitt(nc.vector, at)
    vt = mk.mark(nc.vector.tensor_tensor(
        _r3(s1, nb), _r3(s1, nb), _r3(s3, nb, 1).to_broadcast([128, nb, 16]), ALU.mult),
        "vector")
    mk.waitt(nc.vector, vt)
    t = mk.mark(nc.vector.tensor_scalar(qacc, s1, 0.25, None, ALU.is_gt), "vector")
    for th in (0.75, 1.25, 1.75):
        mk.waitt(nc.vector, t)
        t = mk.mark(nc.vector.tensor_scalar(tq, s1, th, None, ALU.is_gt), "vector")
        mk.waitt(nc.vector, t)
        t = mk.mark(nc.vector.tensor_tensor(qacc, qacc, tq, ALU.add), "vector")
    mk.waitt(nc.vector, t)
    t = mk.mark(nc.vector.tensor_scalar(qacc, qacc, 0.5, None, ALU.mult), "vector")
    for th, w in ((2.5, 1.0), (3.5, 1.0), (5.0, 2.0)):
        mk.waitt(nc.vector, t)
        t = mk.mark(nc.vector.tensor_scalar(tq, s1, th, None, ALU.is_gt), "vector")
        mk.waitt(nc.vector, t)
        if w != 1.0:
            t = mk.mark(nc.vector.tensor_scalar(tq, tq, w, None, ALU.mult), "vector")
            mk.waitt(nc.vector, t)
        t = mk.mark(nc.vector.tensor_tensor(qacc, qacc, tq, ALU.add), "vector")
    mk.waitt(nc.scalar, t)
    at = mk.mark(nc.scalar.activation(out=tq, in_=in_ap, func=AF.Sign), "scalar")
    mk.waitt(nc.vector, at)
    t = mk.mark(nc.vector.tensor_tensor(qacc, qacc, tq, ALU.mult), "vector")
    mk.waitt(nc.vector, t)
    t = mk.mark(nc.vector.tensor_tensor(
        _r3(out_ap, nb), _r3(qacc, nb), _r3(s2, nb, 1).to_broadcast([128, nb, 16]),
        ALU.mult), "vector")
    mk.waitt(nc.vector, t)
    return t


def emit_l1_ffn(mk: MK, slot, x_tiles):
    nc = mk.nc
    h4 = mk.sb("hbuf", [128, PT * S], BF16)
    emit_stats(mk, x_tiles, LN_EPS, "cur")
    emit_modln(mk, x_tiles, "cur", 1, 4, 3, slot, h4)
    if mk.enable_taps:
        mk.tap("h4t", h4[:, 0:S])
    wgd = mk.din("w_gate", mk.host["w_gate"], dt=F32)
    wud = mk.din("w_up", mk.host["w_up"], dt=F32)
    wdd = mk.din("w_down", mk.host["w_down"], dt=F32)
    wgqT = mk.dscr("wgqT", [PT, 128, IQ], BF16)
    wuqT = mk.dscr("wuqT", [PT, 128, IQ], BF16)
    wdqT = mk.dscr("wdqT", [IQ // 128, 128, D], BF16)
    wf = mk.scraf(0, 2816)
    wq = mk.scra(5632, 2816)
    tstg = mk.scra(8448, 256)
    mk.barrier()

    def qblk_chunks(out_ap_fn, in_ap_fn, total):
        off = 0
        while off < total:
            cw = min(1024, total - off)
            emit_qblk(mk, out_ap_fn(off, cw), in_ap_fn(off, cw), cw)
            off += cw

    for wsrc, wdst in ((wgd, wgqT), (wud, wuqT)):
        for r in range(IQ // 128):
            tw = mk.dma(nc.sync, wf[:, 0:1024], wsrc[r * 128:(r + 1) * 128, :])
            mk.waitt(nc.scalar, tw)
            emit_qblk(mk, wq[:, 0:1024], wf[:, 0:1024], 1024)
            mk.barrier()
            for ft in range(PT):
                tt = mk.dmaT(nc.scalar, tstg[:, (ft % 2) * 128:(ft % 2) * 128 + 128],
                             wq[:, ft * 128:(ft + 1) * 128])
                mk.waitt(nc.sync, tt)
                mk.dma(nc.sync, wdst[ft][:, r * 128:(r + 1) * 128],
                       tstg[:, (ft % 2) * 128:(ft % 2) * 128 + 128])
                if ft >= 1:
                    mk.waitt(nc.scalar, ("dma_sync", mk.cnt["dma_sync"] - 16))
            mk.barrier()
    for r in range(PT):
        tw = mk.dma(nc.sync, wf[:, 0:2816], wdd[r * 128:(r + 1) * 128, :])
        mk.waitt(nc.scalar, tw)
        qblk_chunks(lambda o, c: wq[:, o:o + c], lambda o, c: wf[:, o:o + c], IQ)
        mk.barrier()
        for kt in range(IQ // 128):
            tt = mk.dmaT(nc.scalar, tstg[:, (kt % 2) * 128:(kt % 2) * 128 + 128],
                         wq[:, kt * 128:(kt + 1) * 128])
            mk.waitt(nc.sync, tt)
            mk.dma(nc.sync, wdqT[kt][:, r * 128:(r + 1) * 128],
                   tstg[:, (kt % 2) * 128:(kt % 2) * 128 + 128])
            if kt >= 1:
                mk.waitt(nc.scalar, ("dma_sync", mk.cnt["dma_sync"] - 16))
        mk.barrier()
    htm = mk.scra(0, 1024)
    hqm = mk.scra(1024, 1024)
    for c in range(NT):
        for p in range(PT):
            mk.dmaT(nc.scalar, htm[:, p * 128:(p + 1) * 128],
                    h4[:, p * S + c * 128:p * S + (c + 1) * 128])
        mk.barrier()
        emit_qblk(mk, hqm[:], htm[:], 1024)
        mk.barrier()
        for p in range(PT):
            mk.dmaT(nc.scalar, h4[:, p * S + c * 128:p * S + (c + 1) * 128],
                    hqm[:, p * 128:(p + 1) * 128])
        mk.barrier()
    NIC = 6
    wsl = mk.scra(0, 2 * 512)
    hq_tm = mk.scra(1024, 2816)
    hqq = mk.scra(3840, 2816)
    gstg = mk.scraf(3328, 512)
    hqf = mk.scra(7680, 2816)
    wdsl = mk.scra(10496, 1024)
    wst2 = mk.scraf(5760, 128)
    for tt_ in range(NT):
        for icc in range(NIC):
            iw = 512 if icc < 5 else 256
            io = icc * 512
            for which, wsrc in ((0, wgqT), (1, wuqT)):
                bank = ("pA", "pB")[which]
                pb = mk.ps_bank(bank)
                mk.waitt(nc.tensor, mk.banks[bank]["r"])
                mmtok = {}
                last = None
                for p in range(PT):
                    mk.waitt(nc.sync, mmtok.get(p - 2))
                    tw = mk.dma(nc.sync, wsl[:, (p % 2) * 512:(p % 2) * 512 + iw],
                                wsrc[p][:, io:io + iw])
                    mk.waitt(nc.tensor, tw)
                    last = nc.tensor.matmul(
                        pb[:, 0:iw], h4[:, p * S + tt_ * 128:p * S + (tt_ + 1) * 128],
                        wsl[:, (p % 2) * 512:(p % 2) * 512 + iw],
                        start=(p == 0), stop=(p == PT - 1))
                    mmtok[p] = mk.mark(last, "tensor")
                mk.banks[bank]["w"] = mmtok[PT - 1]
                mk.barrier()
            pa, pb = mk.ps_bank("pA"), mk.ps_bank("pB")
            mk.waitt(nc.scalar, mk.banks["pA"]["w"])
            at = mk.mark(nc.scalar.activation(out=gstg[:, 0:iw], in_=pa[:, 0:iw],
                                              func=AF.Silu), "scalar")
            mk.banks["pA"]["r"] = at
            mk.waitt(nc.vector, at)
            mk.waitt(nc.vector, mk.banks["pB"]["w"])
            vt = mk.mark(nc.vector.tensor_tensor(hq_tm[:, io:io + iw], gstg[:, 0:iw],
                                                 pb[:, 0:iw], ALU.mult), "vector")
            mk.banks["pB"]["r"] = vt
            mk.barrier()
        qblk_chunks(lambda o, c: hqq[:, o:o + c], lambda o, c: hq_tm[:, o:o + c], IQ)
        mk.barrier()
        for kt in range(IQ // 128):
            mk.dmaT(nc.scalar, hqf[:, kt * 128:(kt + 1) * 128],
                    hqq[:, kt * 128:(kt + 1) * 128])
        mk.barrier()
        for kt in range(IQ // 128):
            tw = mk.dma(nc.sync, wdsl[:], wdqT[kt])
            mk.waitt(nc.tensor, tw)
            for mc in range(PT):
                bank = ("pE", "pF")[mc // 4]
                pb = mk.ps_bank(bank)
                if kt == 0 and mc % 4 == 0:
                    mk.waitt(nc.tensor, mk.banks[bank]["r"])
                last = nc.tensor.matmul(pb[:, (mc % 4) * 128:(mc % 4) * 128 + 128],
                                        wdsl[:, mc * 128:(mc + 1) * 128],
                                        hqf[:, kt * 128:(kt + 1) * 128],
                                        start=(kt == 0), stop=(kt == IQ // 128 - 1))
            tok = mk.mark(last, "tensor")
            mk.banks["pE"]["w"] = tok
            mk.banks["pF"]["w"] = tok
            mk.barrier()
        for mc in range(PT):
            bank = ("pE", "pF")[mc // 4]
            pb = mk.ps_bank(bank)
            mk.waitt(nc.scalar, mk.banks[bank]["w"])
            at = mk.mark(nc.scalar.activation(out=wst2, in_=pb[:, (mc % 4) * 128:(mc % 4) * 128 + 128],
                                              func=AF.Identity,
                                              scale=pp(mk, 1, 5, mc, slot)), "scalar")
            mk.banks[bank]["r"] = at
            mk.waitt(nc.vector, at)
            mk.mark(nc.vector.tensor_tensor(
                x_tiles[mc][:, tt_ * 128:(tt_ + 1) * 128],
                x_tiles[mc][:, tt_ * 128:(tt_ + 1) * 128], wst2, ALU.add), "vector")
            mk.barrier()
    if mk.enable_taps:
        mk.tap("x4t0", x_tiles[0])



def build_program(inputs, dev_slots=None, taps=False):
    h, pc, perms = host_prepare(inputs)
    nc = bass.Bass("TRN2", target_bir_lowering=False)
    ctx = ExitStack()
    mk = MK(nc, ctx)
    mk.host.update(h)
    mk.host_percore.update(pc)
    mk.perms = perms
    mk.enable_taps = taps
    emit_consts(mk)
    emit_mod_prep(mk)
    emit_attn_prep_consts(mk)
    emit_weight_loads_prefix(mk)
    slots = list(range(NB)) if dev_slots is None else dev_slots
    for slot in slots:
        own = slot == NB - 1
        x_tiles, h2 = emit_prefix_batch(mk, slot, own)
    mk.barrier()
    if taps:
        mk.tap("counts", mk.bufs["rt_counts"][:])
        mk.tap("offs", mk.bufs["rt_offs"][:])
        mk.tap("zl", mk.bufs["rt_zl"][:])
    if not getattr(mk, "prefix_only", False):
        slot = NB - 1
        emit_moe_dense(mk, slot, h2)
        emit_shared(mk, h2)
        emit_x2(mk, slot, x_tiles)
        emit_l1_mixer(mk, slot, x_tiles)
        emit_l1_ffn(mk, slot, x_tiles)
        outd = mk.dout("out_fm", [PT, 128, S], F32)
        mk.barrier()
        for p in range(PT):
            mk.dma(nc.sync, outd[p], x_tiles[p])
    for qn in mk.engq.values():
        if qn in mk.cnt:
            nc.gpsimd.wait_ge(mk.sem(qn), mk.cnt[qn])
    mk.barrier()
    return nc, mk


def run_cores(nc, mk, core_ids):
    in_maps = []
    for b in core_ids:
        m = {}
        for k, v in mk.host.items():
            if k in mk.dram:
                m[k] = v
        for k, lst in mk.host_percore.items():
            if k in mk.dram:
                m[k] = lst[b]
        in_maps.append(m)
    res = run_bass_kernel_spmd(nc, in_maps, core_ids=list(range(len(core_ids))))
    return res


def kernel(**inputs):
    nc, mk = build_program(inputs, dev_slots=None, taps=False)
    res = run_cores(nc, mk, list(range(NB)))
    out = np.empty((NB, S, D), np.float32)
    for b in range(NB):
        arr = res.results[b]["out_fm"]
        out[b] = np.ascontiguousarray(np.transpose(arr, (2, 0, 1)).reshape(S, D))
    return out
